# version 16
# baseline (speedup 1.0000x reference)
import numpy as np
import ml_dtypes
from contextlib import ExitStack

import concourse.bass as bass
import concourse.mybir as mybir
from concourse.bass_utils import run_bass_kernel_spmd

F32 = mybir.dt.float32
BF16 = mybir.dt.bfloat16
AF = mybir.ActivationFunctionType
ALU = mybir.AluOpType

H = 16
D = 128
KC = 16
IC = 64
J = 1024
TH = 512
EPS_DEN = 1e-6
EPS_LN = 1e-5


class Prog:

    ENGS = ("tensor", "vector", "scalar", "sync", "gpsimd")
    ESEM = {"tensor": "MM", "vector": "EV", "scalar": "AC", "gpsimd": "GP"}

    def __init__(self, simsafe=False):
        self.seq = {e: [] for e in self.ENGS}
        self.cnt = {}
        self.simsafe = simsafe

    def val(self, sem):
        return self.cnt.get(sem, 0)

    def add(self, eng, emit, waits=(), incs=()):
        waits = list(waits)
        incs = list(incs)
        if self.simsafe and eng in self.ESEM:
            es = self.ESEM[eng]
            waits.append((es, self.val(es)))
            if not incs:
                incs.append((es, 1))
        self.seq[eng].append((emit, waits, incs))
        for s, a in incs:
            self.cnt[s] = self.val(s) + a
        return {s: self.val(s) for s, _ in incs}

    def emit_all(self, block, sems):
        def mk(eng_name):
            def body(eng):
                for emit, waits, incs in self.seq[eng_name]:
                    for s, v in waits:
                        if v > 0:
                            eng.wait_ge(sems[s], int(v))
                    inst = emit(eng)
                    for s, a in incs:
                        if inst is None:
                            eng.nop().then_inc(sems[s], a)
                        else:
                            inst = inst.then_inc(sems[s], a)
            return body
        block.tensor(mk("tensor"))
        block.vector(mk("vector"))
        block.scalar(mk("scalar"))
        block.sync(mk("sync"))
        block.gpsimd(mk("gpsimd"))


def build(stage="full", ncores=8, simsafe=False, detect_races=False):
    nc = bass.Bass("TRN2", target_bir_lowering=False, detect_race_conditions=detect_races)

    x_d = nc.dram_tensor("x", [KC, 128, J], F32, kind="ExternalInput").ap()
    wqkv_d = nc.dram_tensor("wqkv", [H, 3, KC, 128, 128], F32, kind="ExternalInput").ap()
    wo_d = nc.dram_tensor("wo", [KC, KC, 128, 128], F32, kind="ExternalInput").ap()
    wg_d = nc.dram_tensor("wg", [IC, KC, 128, 128], F32, kind="ExternalInput").ap()
    wu_d = nc.dram_tensor("wu", [IC, KC, 128, 128], F32, kind="ExternalInput").ap()
    wd_d = nc.dram_tensor("wd", [KC, IC, 128, 128], F32, kind="ExternalInput").ap()
    dpbc_d = nc.dram_tensor("dpbc", [128, H], F32, kind="ExternalInput").ap()
    dp16_d = nc.dram_tensor("dp16", [16, 1], F32, kind="ExternalInput").ap()
    flag_d = nc.dram_tensor("flag", [128, 1], F32, kind="ExternalInput").ap()
    iota_d = nc.dram_tensor("iota", [128, J], F32, kind="ExternalInput").ap()
    ehc_d = nc.dram_tensor("ehc", [H, 128, H], BF16, kind="ExternalInput").ap()
    ohr_d = nc.dram_tensor("ohr", [H, H, 128], BF16, kind="ExternalInput").ap()
    out_d = nc.dram_tensor("out", [KC, 128, J], F32, kind="ExternalOutput").ap()

    _bat = [(0, 16), (16, 14), (30, 2)]
    cc_ins = [nc.dram_tensor(f"cc_in{i}", [128, n], F32, kind="Internal")
              for i, (c0, n) in enumerate(_bat)]
    cc_outs = [nc.dram_tensor(f"cc_out{i}", [2, 128, n], F32, kind="Internal")
               for i, (c0, n) in enumerate(_bat)]

    p = Prog(simsafe=simsafe)
    SEM_NAMES = ["LC", "LX0", "LX1", "LX2", "LX3", "LX4", "LX5", "LX6", "LX7", "LKV", "LWA", "LWB", "LW0", "LW1", "LW2", "LW3", "LRA", "LRB", "CD0", "CD1", "CD2", "CR0", "CR1", "CR2", "CS", "MM", "EV", "AC", "GP", "ST"]

    with ExitStack() as ctx:
        e = ctx.enter_context
        PA = e(nc.sbuf_tensor("PA", [128, KC, J], F32))
        PB = e(nc.sbuf_tensor("PB", [128, KC, J], BF16))
        PC = e(nc.sbuf_tensor("PC", [128, KC, J], BF16))
        TF = e(nc.sbuf_tensor("TF", [128, 3, J], F32))
        DS = e(nc.sbuf_tensor("DS", [128, 2, J], F32))
        T4 = e(nc.sbuf_tensor("T4", [128, 2, J], BF16))
        WQKV = e(nc.sbuf_tensor("WQKV", [128, 2, 3, KC, 128], BF16))
        WD2 = e(nc.sbuf_tensor("WD2", [128, 2, KC, 128], BF16))
        XR = e(nc.sbuf_tensor("XR", [128, 2, J], F32))
        IOTA = e(nc.sbuf_tensor("IOTA", [128, J], F32))
        R16 = e(nc.sbuf_tensor("R16", [16, J], F32))
        R16B = e(nc.sbuf_tensor("R16B", [16, J], BF16))
        SI = e(nc.sbuf_tensor("SI", [128, 4 * H], F32))
        SEND = SI[:, 0:2 * H]
        INIT = SI[:, 2 * H:4 * H]
        INITB = e(nc.sbuf_tensor("INITB", [128, H], BF16))
        OH2 = e(nc.sbuf_tensor("OH2", [128, H, H], BF16))
        DL16 = e(nc.sbuf_tensor("DL16", [16, 2], F32))
        DEC16 = DL16[:, 0:1]
        LND16 = DL16[:, 1:2]
        EH = e(nc.sbuf_tensor("EH", [128, H, H], BF16))
        OHR = e(nc.sbuf_tensor("OHR", [16, H, 128], BF16))
        DPB = e(nc.sbuf_tensor("DPB", [128, H], F32))
        DEC = DPB
        LND = e(nc.sbuf_tensor("LND", [128, H], F32))
        FLAG = e(nc.sbuf_tensor("FLAG", [128, 1], F32))
        ONE_C = e(nc.sbuf_tensor("ONE_C", [128, 1], BF16))
        ONE_R = e(nc.sbuf_tensor("ONE_R", [1, 128], BF16))
        CONSTS = e(nc.sbuf_tensor("CONSTS", [128, 3], F32))
        EPSL = CONSTS[:, 0:1]
        NEGH = CONSTS[:, 1:2]
        NEG1 = CONSTS[:, 2:3]

        TFb = TF[:, :, :].bitcast(BF16)
        DSb = DS[:, :, :].bitcast(BF16)
        XRb = XR[:, :, :].bitcast(BF16)
        T5 = [XRb[:, 0, 0:J], XRb[:, 0, J:2 * J]]

        def bf_scratch(i):
            if i < 6:
                return TFb[:, i // 2, (i % 2) * J:(i % 2) * J + J]
            if i < 10:
                j = i - 6
                return DSb[:, j // 2, (j % 2) * J:(j % 2) * J + J]
            return T4[:, i - 10, :]

        P0 = e(nc.psum_tensor("P0", [128, J], F32))
        P1 = e(nc.psum_tensor("P1", [128, J], F32))
        P2 = e(nc.psum_tensor("P2", [128, J], F32))
        P3 = e(nc.psum_tensor("P3", [128, J], F32))
        PS = [P0, P1, P2, P3]

        sems = {}
        for s in SEM_NAMES:
            sems[s] = e(nc.semaphore(s))
        _sems = sems
        block = e(nc.Block())

        def mm_group(psum, stat_fn, mov_fn, nk, waits, inc_sem="MM",
                     start=True, stop=True):
            def emit(t):
                inst = None
                for k in range(nk):
                    st = stat_fn(k)
                    for th in range(2):
                        inst = t.matmul(
                            psum[:, th * TH:(th + 1) * TH],
                            st, mov_fn(k, th),
                            start=(start and k == 0), stop=(stop and k == nk - 1))
                return inst
            return p.add("tensor", emit, waits=waits, incs=[(inc_sem, 1)])

        def vop(emit, waits=(), inc=True):
            return p.add("vector", emit, waits=waits,
                         incs=[("EV", 1)] if inc else [])

        def aop(emit, waits=(), inc=True):
            return p.add("scalar", emit, waits=waits,
                         incs=[("AC", 1)] if inc else [])

        def dma(queue, out, in_, sem, waits=()):
            return p.add(queue, lambda g: g.dma_start(out=out, in_=in_),
                         waits=waits, incs=[(sem, 16)])

        dma("sync", DPB[:, :], dpbc_d, "LC")
        dma("sync", DEC16, dp16_d, "LC")
        dma("sync", FLAG[:, :], flag_d, "LC")
        dma("sync", IOTA[:, :], iota_d, "LC")
        dma("sync", EH[:, :, :], ehc_d.rearrange("h p c -> p h c"), "LC")
        dma("sync", OHR[:, :, :], ohr_d.rearrange("h p c -> p h c"), "LC")
        lr_pre = p.val("LC")

        dma("gpsimd", WQKV[:, 0, 0, :, :], wqkv_d[0, 0].rearrange("k p m -> p k m"), "LWA")
        for i in range(8):
            dma("gpsimd", PB[:, 2 * i:2 * i + 2, :],
                x_d.rearrange("c p m -> p c m")[:, 2 * i:2 * i + 2, :], f"LX{i}")
        dma("gpsimd", WQKV[:, 0, 1:3, :, :],
            wqkv_d[0, 1:3].rearrange("j k p m -> p j k m"), "LKV")

        vop(lambda v: v.memset(ONE_C[:, :], 1.0), inc=False)
        vop(lambda v: v.memset(ONE_R[:, :], 1.0), inc=False)
        vop(lambda v: v.memset(EPSL, EPS_LN), inc=False)
        vop(lambda v: v.memset(NEGH, -0.5), inc=False)
        vop(lambda v: v.memset(NEG1, -1.0), inc=False)
        ev_oh2z = vop(lambda v: v.memset(OH2[:, :, :], 0.0))["EV"]
        ac_sig = aop(lambda s: s.activation(DEC[:, :], DPB[:, :], AF.Sigmoid),
                     waits=[("LC", lr_pre)])["AC"]
        ac_dec = aop(lambda s: s.activation(LND[:, :], DEC[:, :], AF.Ln),
                     waits=[("AC", ac_sig)])["AC"]
        aop(lambda s: s.activation(DEC16, DEC16, AF.Sigmoid))
        aop(lambda s: s.activation(LND16, DEC16, AF.Ln))
        ac_dp16 = aop(lambda s: s.activation(R16[:, :], IOTA[0:16, :], AF.Exp,
                                             scale=LND16))["AC"]

        mm_q = [0] * H; mm_k = [0] * H; mm_v = [0] * H
        mm_den = [0] * H
        ev_qphi = [0] * H; ev_kphi = [0] * H; ev_xkv = [0] * H
        ev_p = [0] * H; ev_state = [0] * H
        lw_head = [0] * H
        mm_last_of_head = [0] * H

        ac_dpow = [0] * H
        ev_rq = [0] * H; ev_kvc = [0] * H; ev_oh = [0] * H
        mm_denc = [0] * H
        ev_init = [0, 0, 0]
        ev_initb = [0, 0, 0]

        def _bi(h):
            return 0 if h < 8 else (1 if h < 15 else 2)

        def emit_a3a(h):
            hb = h % 2
            half = _bi(h)
            w = [("AC", ac_dec), ("LC", lr_pre)]
            if h >= 2:
                w.append(("EV", ev_kvc[h - 2]))
            ac_dpow[h] = aop(lambda s, h=h, hb=hb: s.activation(
                DS[:, hb, :], IOTA[:, :], AF.Exp, scale=LND[:, h:h + 1]), waits=w)["AC"]
            if h < 8:
                ev_kvc[h] = vop(lambda v, h=h, hb=hb: v.scalar_tensor_tensor(
                    PA[:, h, :], DS[:, hb, :], INIT[:, 2 * h:2 * h + 1],
                    PA[:, h, :], ALU.mult, ALU.add),
                    waits=[("AC", ac_dpow[h]), ("EV", ev_init[half])])["EV"]
            else:
                sl = 2 if h % 2 == 0 else 0
                w_kt = [("AC", ac_dpow[h]), ("EV", ev_init[half])]
                if h >= 10:
                    w_kt.append(("EV", ev_kvc[h - 2]))
                ac_kt = aop(lambda s, h=h, hb=hb, sl=sl: s.activation(
                    TF[:, sl, :], DS[:, hb, :], AF.Copy,
                    scale=INIT[:, 2 * h:2 * h + 1]), waits=w_kt)["AC"]
                ev_kvc[h] = vop(lambda v, h=h, sl=sl: v.tensor_tensor(
                    PA[:, h, :], PA[:, h, :], TF[:, sl, :], ALU.add),
                    waits=[("AC", ac_kt)])["EV"]

        ev_oh2 = [0, 0, 0]

        def emit_oh2(batch):
            c0, n = _bat[batch]
            for h in range(c0 // 2, (c0 + n) // 2):
                last = (h == (c0 + n) // 2 - 1)
                vop(lambda v, h=h: v.tensor_copy(
                    OH2[:, h, h:h + 1], INITB[:, h:h + 1]),
                    waits=[("EV", ev_initb[batch]), ("EV", ev_oh2z)], inc=last)
            ev_oh2[batch] = p.val("EV")

        def emit_denc(h):
            def denc_emit(t, h=h):
                inst = None
                for th in range(2):
                    inst = t.matmul(P2[0:16, th * TH:(th + 1) * TH],
                                    OH2[:, h, :], PC[:, h, th * TH:(th + 1) * TH],
                                    start=(h == 0), stop=(h == H - 1))
                return inst
            mm_denc[h] = p.add("tensor", denc_emit,
                               waits=[("EV", ev_oh2[_bi(h)])],
                               incs=[("MM", 1)])["MM"]

        def emit_cc(batch, w_state):
            c0, n = _bat[batch]
            dma("sync", cc_ins[batch].ap(), SEND[:, c0:c0 + n], f"CD{batch}",
                waits=[("EV", w_state)])
            p.add("gpsimd", lambda g, batch=batch: g.collective_compute(
                "AllGather", ALU.bypass,
                replica_groups=[[i, i + 1] for i in range(0, ncores, 2)],
                ins=[cc_ins[batch].ap()], outs=[cc_outs[batch].ap()]),
                waits=[(f"CD{batch}", 16)], incs=[("CS", 1)])

        def emit_cc_recv(batch):
            c0, n = _bat[batch]
            dma("sync", INIT[:, c0:c0 + n], cc_outs[batch].ap()[0, :, :],
                f"CR{batch}", waits=[("CS", batch + 1)])

        def emit_init(batch):
            c0, n = _bat[batch]
            ev_init[batch] = vop(lambda v, c0=c0, n=n: v.tensor_scalar(
                INIT[:, c0:c0 + n], INIT[:, c0:c0 + n],
                FLAG[:, :], None, ALU.mult),
                waits=[(f"CR{batch}", 16), ("LC", lr_pre)])["EV"]
            ev_initb[batch] = vop(lambda v, c0=c0, n=n: v.tensor_copy(
                INITB[:, c0 // 2:(c0 + n) // 2],
                INIT[:, c0 + 1:c0 + n:2]))["EV"]

        for h in range(H):
            hb = h % 2
            lwsem = "LWA" if hb == 0 else "LWB"
            if h > 0 and h != 15:
                waits = [("MM", mm_last_of_head[h - 2])] if h >= 2 else []
                dma("gpsimd", WQKV[:, hb, :, :, :],
                    wqkv_d[h].rearrange("j k p m -> p j k m"), lwsem, waits=waits)
            lw_head[h] = p.val(lwsem)

            stat = lambda j, hb=hb: (lambda k: WQKV[:, hb, j, k, :])
            mov = lambda k, th: PB[:, k, th * TH:(th + 1) * TH]

            if h == 0:
                def q0_emit(t, sems_local=None):
                    inst = None
                    for k in range(KC):
                        if k % 2 == 0:
                            t.wait_ge(_sems[f"LX{k // 2}"], 16)
                        st0 = WQKV[:, 0, 0, k, :]
                        for th in range(2):
                            inst = t.matmul(P0[:, th * TH:(th + 1) * TH], st0,
                                            mov(k, th),
                                            start=(k == 0), stop=(k == KC - 1))
                    return inst
                mm_q[h] = p.add("tensor", q0_emit, waits=[("LWA", 16)],
                                incs=[("MM", 1)])["MM"]
            else:
                w_q = [("LWA" if hb == 0 else "LWB", lw_head[h])]
                w_q.append(("EV", ev_qphi[h - 1]))
                mm_q[h] = mm_group(P0, stat(0), mov, KC, w_q)["MM"]
            w_k = [("EV", ev_kphi[h - 1])] if h >= 1 else [("LKV", 16)]
            mm_k[h] = mm_group(P1, stat(1), mov, KC, w_k)["MM"]
            w_v = [("EV", ev_xkv[h - 1])] if h >= 1 else []
            mm_v[h] = mm_group(P2, stat(2), mov, KC, w_v)["MM"]

            ev1 = vop(lambda v: v.tensor_scalar(TF[:, 0, :], P0[:, :], 0.0, None, ALU.min),
                      waits=[("MM", mm_q[h])])["EV"]
            ac1 = aop(lambda s: s.activation(TF[:, 1, :], TF[:, 0, :], AF.Exp),
                      waits=[("EV", ev1)])["AC"]
            ev_qphi[h] = vop(
                lambda v, h=h: v.scalar_tensor_tensor(
                    PC[:, h, :], P0[:, :], 0.0, TF[:, 1, :], ALU.max, ALU.add),
                waits=[("AC", ac1)])["EV"]
            ev2 = vop(lambda v: v.tensor_scalar(TF[:, 2, :], P1[:, :], 0.0, None, ALU.min),
                      waits=[("MM", mm_k[h])])["EV"]
            ac2 = aop(lambda s: s.activation(TF[:, 1, :], TF[:, 2, :], AF.Exp),
                      waits=[("EV", ev2)])["AC"]
            ev_kphi[h] = vop(
                lambda v: v.scalar_tensor_tensor(
                    TF[:, 0, :], P1[:, :], 0.0, TF[:, 1, :], ALU.max, ALU.add),
                waits=[("AC", ac2)])["EV"]
            ev_xkv[h] = vop(
                lambda v: v.tensor_tensor(TF[:, 2, :], TF[:, 0, :], P2[:, :], ALU.mult),
                waits=[("MM", mm_v[h])])["EV"]
            vop(lambda v, h=h: v.tensor_tensor_scan(
                PA[:, h, :], DEC[:, h:h + 1].broadcast_to([128, J]), TF[:, 2, :],
                0.0, ALU.mult, ALU.add),
                waits=[("AC", ac_dec)])
            vop(lambda v, h=h: v.tensor_tensor_scan(
                TF[:, 1, :], DEC[:, h:h + 1].broadcast_to([128, J]), TF[:, 0, :],
                0.0, ALU.mult, ALU.add))
            w_p = [("MM", mm_den[h - 2])] if h >= 2 else []
            ev_p[h] = vop(lambda v, h=h, hb=hb: v.tensor_tensor(
                T4[:, hb, :], PC[:, h, :], TF[:, 1, :], ALU.mult), waits=w_p)["EV"]
            vop(lambda v, h=h: v.tensor_copy(SEND[:, 2 * h:2 * h + 1],
                                             PA[:, h, J - 1:J]), inc=False)
            ev_state[h] = vop(lambda v, h=h: v.tensor_copy(
                SEND[:, 2 * h + 1:2 * h + 2], TF[:, 1, J - 1:J]))["EV"]
            def mk_den(h):
                def den_emit(t, h=h):
                    inst = None
                    for th in range(2):
                        inst = t.matmul(P3[0:16, th * TH:(th + 1) * TH],
                                        EH[:, h, :], T4[:, h % 2, th * TH:(th + 1) * TH],
                                        start=(h == 0), stop=(h == H - 1))
                    return inst
                return den_emit
            if h >= 1:
                mm_den[h - 1] = p.add("tensor", mk_den(h - 1),
                                      waits=[("EV", ev_p[h - 1])], incs=[("MM", 1)])["MM"]
            if h == H - 1:
                mm_den[h] = p.add("tensor", mk_den(h),
                                  waits=[("EV", ev_p[h])], incs=[("MM", 1)])["MM"]
            mm_last_of_head[h] = p.val("MM")

            if h == 8:
                emit_cc(0, ev_state[7])
            elif h == 10:
                emit_cc_recv(0)
            elif h == 11:
                emit_init(0)
                emit_oh2(0)
            elif h == 14:
                dma("gpsimd", WQKV[:, 1, :, :, :],
                    wqkv_d[15].rearrange("j k p m -> p j k m"), "LWB",
                    waits=[("MM", mm_last_of_head[13])])
                lw_head[15] = p.val("LWB")
                emit_cc(1, ev_state[14])
                emit_a3a(4)
                emit_a3a(5)
            elif h >= 12:
                j0 = 2 * (h - 12) if h < 14 else 6
                emit_a3a(j0)
                emit_a3a(j0 + 1)

        emit_cc(2, ev_state[15])

        mm_wo = [0] * KC
        ev_r = [0] * KC
        lw_wo = [0] * KC
        def wo_buf(m):
            r = m % 4
            return WQKV[:, r % 2, r // 2, :, :]
        for m in range(4):
            dma("gpsimd", wo_buf(m),
                wo_d[m].rearrange("k p m -> p k m"), f"LW{m % 4}",
                waits=[("MM", mm_last_of_head[H - 1])])
            lw_wo[m] = p.val(f"LW{m % 4}")

        for j in range(0, 8):
            emit_denc(j)
        emit_cc_recv(1)
        emit_init(1)
        emit_oh2(1)
        for j in range(8, 15):
            emit_a3a(j)
        for j in range(8, 15):
            emit_denc(j)
        emit_cc_recv(2)
        emit_init(2)
        emit_oh2(2)
        emit_a3a(15)
        emit_denc(15)

        DS16 = DS[0:16, 0, :]
        vop(lambda v: v.tensor_tensor(DS16, R16[:, :], P2[0:16, :], ALU.mult),
            waits=[("MM", mm_denc[H - 1]), ("AC", ac_dp16)], inc=False)
        vop(lambda v: v.tensor_tensor(DS16, DS16, P3[0:16, :], ALU.add), inc=False)
        ev_dclip = vop(lambda v: v.tensor_scalar(DS16, DS16, EPS_DEN, None,
                                                 ALU.max))["EV"]
        aop(lambda s: s.activation(DS16, DS16, AF.Ln), waits=[("EV", ev_dclip)])
        ac_rden = aop(lambda s: s.activation(R16B[:, :], DS16, AF.Exp,
                                             scale=NEG1[0:16]))["AC"]
        ev_rden = None

        mm_rbc = [0] * H
        ev_oa = [0] * H
        gp_t2 = [0] * H
        for h in range(H):
            hb = h % 2
            Pr = PS[hb]
            w = [("AC", ac_rden)]
            if h >= 2:
                w.append(("EV", ev_oa[h - 2]))
            def rbc_emit(t, h=h, Pr=Pr):
                inst = None
                for th in range(2):
                    inst = t.matmul(Pr[:, th * TH:(th + 1) * TH],
                                    OHR[:, h, :], R16B[:, th * TH:(th + 1) * TH],
                                    start=True, stop=True)
                return inst
            mm_rbc[h] = p.add("tensor", rbc_emit, waits=w, incs=[("MM", 1)])["MM"]
            gp_t2[h] = p.add("gpsimd", lambda g, h=h: g.tensor_tensor(
                PB[:, h, :], PA[:, h, :], PC[:, h, :], ALU.mult),
                waits=[("EV", ev_kvc[h]),
                       ("MM", mm_last_of_head[H - 1])],
                incs=[("GP", 1)])["GP"]
            ev_oa[h] = vop(lambda v, h=h, Pr=Pr: v.tensor_tensor(
                PB[:, h, :], PB[:, h, :], Pr[:, :], ALU.mult),
                waits=[("MM", mm_rbc[h]), ("GP", gp_t2[h])])["EV"]

        lr_x = [0] * KC
        mm_ln1 = [0] * KC
        ac_sq = [0] * KC
        for m in range(KC):
            lwsem = f"LW{m % 4}"
            if m >= 4:
                dma("gpsimd", wo_buf(m),
                    wo_d[m].rearrange("k p m -> p k m"), lwsem,
                    waits=[("MM", mm_wo[m - 4])])
                lw_wo[m] = p.val(lwsem)
            lrsem = "LRA" if m % 2 == 0 else "LRB"
            if m >= 2:
                w = [("EV", ev_r[m - 2])]
            else:
                w = [("MM", mm_denc[H - 1])]
            dma("sync", XR[:, m % 2, :], x_d[m], lrsem, waits=w)
            lr_x[m] = p.val(lrsem)

            Pw = PS[m % 2]
            w = [(lwsem, lw_wo[m]), ("EV", ev_oa[H - 1])]
            if m >= 2:
                w.append(("EV", ev_r[m - 2]))
            mm_wo[m] = mm_group(
                Pw, lambda k, m=m: wo_buf(m)[:, k, :],
                lambda k, th: PB[:, k, th * TH:(th + 1) * TH], KC, w)["MM"]
            vop(lambda v, m=m, Pw=Pw: v.tensor_tensor(
                PA[:, m, :], Pw[:, :], XR[:, m % 2, :], ALU.add),
                waits=[("MM", mm_wo[m]), ("LRA" if m % 2 == 0 else "LRB", lr_x[m])], inc=False)
            ev_r[m] = vop(lambda v, m=m: v.tensor_copy(PC[:, m, :], PA[:, m, :]))["EV"]
            w = [("EV", ev_r[m])]
            if m >= 2:
                w.append(("MM", mm_ln1[m - 2]))
            ac_sq[m] = aop(lambda s, m=m: s.activation(
                T4[:, m % 2, :], PA[:, m, :], AF.Square), waits=w)["AC"]
            def mk_ln1(m):
                def ln1_emit(t, m=m):
                    inst = None
                    for th in range(2):
                        t.matmul(P2[0:1, th * TH:(th + 1) * TH], ONE_C[:, :],
                                 PC[:, m, th * TH:(th + 1) * TH],
                                 start=(m == 0), stop=(m == KC - 1))
                        inst = t.matmul(P3[0:1, th * TH:(th + 1) * TH], ONE_C[:, :],
                                        T4[:, m % 2, th * TH:(th + 1) * TH],
                                        start=(m == 0), stop=(m == KC - 1))
                    return inst
                return ln1_emit
            if m >= 1:
                mm_ln1[m - 1] = p.add(
                    "tensor", mk_ln1(m - 1),
                    waits=[("EV", ev_r[m - 1]), ("AC", ac_sq[m - 1])],
                    incs=[("MM", 1)])["MM"]
            if m == KC - 1:
                mm_ln1[m] = p.add(
                    "tensor", mk_ln1(m),
                    waits=[("EV", ev_r[m]), ("AC", ac_sq[m])],
                    incs=[("MM", 1)])["MM"]

        def ln_rows_and_bcast(mm_wait, Psum_s, Psum_q, tag):
            mean = DS[0:1, 0, :]
            ex2 = DS[0:1, 1, :]
            rstd = R16[0:1, :]
            a_row = T4[0:1, 0, :]
            b_row = T4[0:1, 1, :]
            vop(lambda v: v.tensor_scalar(mean, Psum_s[0:1, :],
                                          1.0 / 2048, None, ALU.mult),
                waits=[("MM", mm_wait)], inc=False)
            vop(lambda v: v.tensor_scalar(ex2, Psum_q[0:1, :],
                                          1.0 / 2048, None, ALU.mult), inc=False)
            vop(lambda v: v.tensor_tensor(rstd, mean, mean, ALU.mult), inc=False)
            ev_v = vop(lambda v: v.tensor_tensor(ex2, ex2, rstd, ALU.subtract))["EV"]
            aop(lambda s: s.activation(rstd, ex2, AF.Ln, bias=EPSL[0:1]),
                waits=[("EV", ev_v)])
            aop(lambda s: s.activation(a_row, rstd, AF.Exp,
                                       scale=NEGH[0:1]))
            ac_rs = aop(lambda s: s.activation(b_row, mean, AF.Copy))["AC"]
            def bc_emit(t):
                inst = None
                for th in range(2):
                    t.matmul(Psum_s[:, th * TH:(th + 1) * TH], ONE_R[:, :],
                             a_row[:, th * TH:(th + 1) * TH], start=True, stop=True)
                    inst = t.matmul(Psum_q[:, th * TH:(th + 1) * TH], ONE_R[:, :],
                                    b_row[:, th * TH:(th + 1) * TH],
                                    start=True, stop=True)
                return inst
            return p.add("tensor", bc_emit, waits=[("AC", ac_rs)],
                         incs=[("MM", 1)])["MM"]

        mm_bc1 = ln_rows_and_bcast(mm_ln1[KC - 1], P2, P3, "ln1")

        ev_x2 = [0] * KC
        ac_x2h = [0] * KC
        for m in range(KC):
            vop(lambda v, m=m: v.tensor_tensor(TF[:, 0, :], PA[:, m, :], P3[:, :],
                                               ALU.subtract),
                waits=[("MM", mm_bc1)], inc=False)
            ev_x2[m] = vop(lambda v, m=m: v.tensor_tensor(
                PA[:, m, :], TF[:, 0, :], P2[:, :], ALU.mult))["EV"]
            ac_x2h[m] = aop(lambda s, m=m: s.activation(
                PC[:, m, :], PA[:, m, :], AF.Copy), waits=[("EV", ev_x2[m])])["AC"]

        if stage == "x2":
            st = dma("sync", out_d.rearrange("c p m -> p c m"), PA[:, :, :], "ST",
                     waits=[("EV", ev_x2[KC - 1]), ("AC", ac_x2h[KC - 1])])
            p.add("sync", lambda s: None, waits=[("ST", st["ST"])])
            p.emit_all(block, sems)
            return nc

        mm_gu = [0] * IC
        ev_h = [0] * IC
        ev_mask = [0] * (IC // 4)
        mm_dn = np.zeros((4, KC), dtype=int)
        ev_dn = np.zeros((4, KC), dtype=int)
        lw_gu = [0] * IC
        lw_dn = np.zeros((4, KC), dtype=int)
        ac_silu = [0] * IC
        ac_abs = [0] * IC

        WG = lambda b: WQKV[:, b, 0, :, :]
        WU = lambda b: WQKV[:, b, 1, :, :]

        def WD(r):
            return WQKV[:, r, 2, :, :] if r < 2 else WD2[:, r - 2, :, :]

        for qt in range(4):
            for lc in range(16):
                c = qt * 16 + lc
                cb = c % 2
                lwsem = "LWA" if cb == 0 else "LWB"
                if c >= 2:
                    w = [("MM", mm_gu[c - 2])]
                elif c == 0:
                    w = [("MM", mm_wo[12])]
                else:
                    w = [("MM", mm_wo[13])]
                dma("gpsimd", WG(cb), wg_d[c].rearrange("k p m -> p k m"), lwsem, waits=w)
                wu_w = []
                if c == 0:
                    wu_w = [("MM", mm_wo[14])]
                elif c == 1:
                    wu_w = [("MM", mm_wo[15])]
                dma("gpsimd", WU(cb), wu_d[c].rearrange("k p m -> p k m"), lwsem,
                    waits=wu_w)
                lw_gu[c] = p.val(lwsem)
                m_wd = lc
                if m_wd < 4:
                    if qt == 0:
                        w2 = [("MM", mm_wo[12 + m_wd])]
                    else:
                        w2 = [("MM", mm_dn[qt - 1][KC - 4 + m_wd])]
                    dma("gpsimd", WD(m_wd), wd_d[m_wd, qt * 16:(qt + 1) * 16].rearrange(
                        "c p m -> p c m"), f"LW{m_wd}", waits=w2)
                    lw_dn[qt][m_wd] = p.val(f"LW{m_wd}")

                Pg, Pu = PS[cb * 2], PS[cb * 2 + 1]
                if c == 0:
                    def g0_emit(t, Pp, wsel):
                        inst = None
                        for k in range(KC):
                            t.wait_ge(_sems["AC"], ac_x2h[k])
                            stw = (WG if wsel == 0 else WU)(0)[:, k, :]
                            for th in range(2):
                                inst = t.matmul(Pp[:, th * TH:(th + 1) * TH], stw,
                                                PC[:, k, th * TH:(th + 1) * TH],
                                                start=(k == 0), stop=(k == KC - 1))
                        return inst
                    p.add("tensor", lambda t, Pg=Pg: g0_emit(t, Pg, 0),
                          waits=[(lwsem, lw_gu[c])], incs=[("MM", 1)])
                    mm_gu[c] = p.add("tensor", lambda t, Pu=Pu: g0_emit(t, Pu, 1),
                                     incs=[("MM", 1)])["MM"]
                else:
                    w = [(lwsem, lw_gu[c]), ("AC", ac_x2h[KC - 1])]
                    if c >= 2:
                        w.append(("EV", ev_h[c - 2]))
                    if qt > 0 and lc == 1:
                        w.append(("EV", ev_dn[qt - 1][KC - 1]))
                    if qt == 0 and lc == 1:
                        w.append(("EV", ev_x2[KC - 1]))
                    mm_group(Pg, lambda k, cb=cb: WG(cb)[:, k, :],
                             lambda k, th: PC[:, k, th * TH:(th + 1) * TH], KC, w)
                    mm_gu[c] = mm_group(Pu, lambda k, cb=cb: WU(cb)[:, k, :],
                                        lambda k, th: PC[:, k, th * TH:(th + 1) * TH],
                                        KC, [], start=True, stop=True)["MM"]
                w = [("MM", mm_gu[c])]
                if c >= 2:
                    w.append(("EV", ev_h[c - 2]))
                if lc % 4 == 0 and c > 0:
                    w.append(("EV", ev_mask[c // 4 - 1]))
                ac_silu[c] = aop(lambda s, cb=cb, Pg=Pg: s.activation(
                    DS[:, cb, :], Pg[:, :], AF.Silu), waits=w)["AC"]
                ev_h[c] = vop(lambda v, lc=lc, cb=cb, Pu=Pu: v.tensor_tensor(
                    PB[:, lc, :], DS[:, cb, :], Pu[:, :], ALU.mult),
                    waits=[("AC", ac_silu[c])])["EV"]
                ac_abs[c] = aop(lambda s, lc=lc: s.activation(
                    bf_scratch(lc % 4), PB[:, lc, :], AF.Abs),
                    waits=[("EV", ev_h[c])])["AC"]

                if lc % 4 == 1:
                    w = [("AC", ac_abs[c])]
                    vop(lambda v: v.tensor_tensor(bf_scratch(4), bf_scratch(0), bf_scratch(1), ALU.is_ge), waits=w, inc=False)
                if lc % 4 == 2:
                    w = [("AC", ac_abs[c])]
                    vop(lambda v: v.tensor_tensor(bf_scratch(5), bf_scratch(0), bf_scratch(2), ALU.is_ge), waits=w, inc=False)
                    vop(lambda v: v.tensor_tensor(bf_scratch(10), bf_scratch(1), bf_scratch(2), ALU.is_ge), inc=False)
                    vop(lambda v: v.tensor_tensor(bf_scratch(11), bf_scratch(4), bf_scratch(5), ALU.add), inc=False)
                    vop(lambda v: v.tensor_tensor(bf_scratch(5), bf_scratch(5), bf_scratch(10), ALU.add), inc=False)
                if lc % 4 == 3:
                    quad = c // 4
                    a = [bf_scratch(i) for i in range(4)]
                    c01, s2a, c12, w0a = bf_scratch(4), bf_scratch(5), bf_scratch(10), bf_scratch(11)
                    c03, c13, c23, tw = bf_scratch(6), bf_scratch(7), bf_scratch(8), bf_scratch(9)
                    l0, l1, l2, l3 = lc - 3, lc - 2, lc - 1, lc
                    w = [("AC", ac_abs[c])]
                    vop(lambda v, a=a, c03=c03: v.tensor_tensor(c03, a[0], a[3], ALU.is_ge), waits=w, inc=False)
                    vop(lambda v, a=a, c13=c13: v.tensor_tensor(c13, a[1], a[3], ALU.is_ge), inc=False)
                    vop(lambda v, a=a, c23=c23: v.tensor_tensor(c23, a[2], a[3], ALU.is_ge), inc=False)
                    vop(lambda v: v.tensor_tensor(tw, w0a, c03, ALU.add), inc=False)
                    vop(lambda v, a=a: v.tensor_scalar(a[0], tw, 1.5, None, ALU.is_ge), inc=False)
                    vop(lambda v: v.tensor_tensor(tw, c12, c13, ALU.add), inc=False)
                    vop(lambda v: v.tensor_tensor(tw, tw, c01, ALU.subtract), inc=False)
                    vop(lambda v, a=a: v.tensor_scalar(a[1], tw, 0.5, None, ALU.is_ge), inc=False)
                    vop(lambda v: v.tensor_tensor(tw, c23, s2a, ALU.subtract), inc=False)
                    vop(lambda v, a=a: v.tensor_scalar(a[2], tw, -0.5, None, ALU.is_ge), inc=False)
                    vop(lambda v: v.tensor_tensor(tw, c03, c13, ALU.add), inc=False)
                    vop(lambda v: v.tensor_tensor(tw, tw, c23, ALU.add), inc=False)
                    vop(lambda v, a=a: v.tensor_scalar(a[3], tw, 1.5, None, ALU.is_le), inc=False)
                    for i, li in enumerate((l0, l1, l2, l3)):
                        last = (i == 3)
                        vop(lambda v, li=li, i=i, a=a: v.tensor_tensor(
                            PB[:, li, :], PB[:, li, :], a[i], ALU.mult), inc=last)
                    ev_mask[quad] = p.val("EV")

            for m in range(KC):
                mb = m % 2
                mr = m % 4
                if m >= 4:
                    ldsem = f"LW{mr}"
                    dma("gpsimd", WD(mr), wd_d[m, qt * 16:(qt + 1) * 16].rearrange(
                        "c p m -> p c m"), ldsem, waits=[("MM", mm_dn[qt][m - 4])])
                    lw_dn[qt][m] = p.val(ldsem)
                ldsem = f"LW{mr}"

                Pd = PS[2 + mb]
                w = [(ldsem, lw_dn[qt][m]),
                     ("EV", ev_h[qt * 16 + 15])]
                if m >= 2:
                    w.append(("EV", ev_dn[qt][m - 2]))
                elif qt > 0:
                    w.append(("EV", ev_dn[qt - 1][KC - 1]))
                def dn_emit(t, mr=mr, Pd=Pd, qt=qt):
                    inst = None
                    for k in range(KC):
                        if k % 4 == 0:
                            t.wait_ge(_sems["EV"], int(ev_mask[qt * 4 + k // 4]))
                        stw = WD(mr)[:, k, :]
                        for th in range(2):
                            inst = t.matmul(Pd[:, th * TH:(th + 1) * TH], stw,
                                            PB[:, k, th * TH:(th + 1) * TH],
                                            start=(k == 0), stop=(k == KC - 1))
                    return inst
                mm_dn[qt][m] = p.add("tensor", dn_emit, waits=w,
                                     incs=[("MM", 1)])["MM"]
                ev_dn[qt][m] = vop(lambda v, m=m, Pd=Pd: v.tensor_tensor(
                    PA[:, m, :], PA[:, m, :], Pd[:, :], ALU.add),
                    waits=[("MM", mm_dn[qt][m])])["EV"]

        mm_ln2 = [0] * KC
        ev_r2h = [0] * KC
        ac_sq2 = [0] * KC
        for m in range(KC):
            ev_r2h[m] = vop(lambda v, m=m: v.tensor_copy(PC[:, m, :], PA[:, m, :]),
                            waits=[("EV", ev_dn[3][m])])["EV"]
            w = [("EV", ev_r2h[m])]
            if m >= 2:
                w.append(("MM", mm_ln2[m - 2]))
            ac_sq2[m] = aop(lambda s, m=m: s.activation(
                T4[:, m % 2, :], PA[:, m, :], AF.Square), waits=w)["AC"]
            def mk_ln2(m):
                def ln2_emit(t, m=m):
                    inst = None
                    for th in range(2):
                        t.matmul(P2[0:1, th * TH:(th + 1) * TH], ONE_C[:, :],
                                 PC[:, m, th * TH:(th + 1) * TH],
                                 start=(m == 0), stop=(m == KC - 1))
                        inst = t.matmul(P3[0:1, th * TH:(th + 1) * TH], ONE_C[:, :],
                                        T4[:, m % 2, th * TH:(th + 1) * TH],
                                        start=(m == 0), stop=(m == KC - 1))
                    return inst
                return ln2_emit
            if m >= 1:
                w = [("EV", ev_r2h[m - 1]), ("AC", ac_sq2[m - 1])]
                if m == 1:
                    w.append(("EV", ev_dn[3][KC - 1]))
                mm_ln2[m - 1] = p.add("tensor", mk_ln2(m - 1), waits=w,
                                      incs=[("MM", 1)])["MM"]
            if m == KC - 1:
                mm_ln2[m] = p.add("tensor", mk_ln2(m),
                                  waits=[("EV", ev_r2h[m]), ("AC", ac_sq2[m])],
                                  incs=[("MM", 1)])["MM"]

        mm_bc2 = ln_rows_and_bcast(mm_ln2[KC - 1], P2, P3, "ln2")

        vop(lambda v: v.tensor_copy(TF[:, 1, :], P2[:, :]),
            waits=[("MM", mm_bc2)], inc=False)
        ev_bce = vop(lambda v: v.tensor_copy(TF[:, 2, :], P3[:, :]))["EV"]
        ev_out = [0] * KC
        gp_out = [0] * KC
        st_tot = 0
        for g in range(8):
            for lm in range(2):
                m = g * 2 + lm
                if m < 11:
                    vop(lambda v, m=m: v.tensor_tensor(TF[:, 0, :], PA[:, m, :],
                                                       P3[:, :], ALU.subtract),
                        waits=[("MM", mm_bc2)], inc=False)
                    ev_out[m] = vop(lambda v, m=m: v.tensor_tensor(
                        PA[:, m, :], TF[:, 0, :], P2[:, :], ALU.mult))["EV"]
                else:
                    p.add("gpsimd", lambda g2, m=m: g2.tensor_tensor(
                        DS[:, 0, :], PA[:, m, :], TF[:, 2, :], ALU.subtract),
                        waits=[("EV", ev_bce)])
                    gp_out[m] = p.add("gpsimd", lambda g2, m=m: g2.tensor_tensor(
                        PA[:, m, :], DS[:, 0, :], TF[:, 1, :], ALU.mult),
                        incs=[("GP", 1)])["GP"]
            w = []
            for lm in range(2):
                m = g * 2 + lm
                if m < 11:
                    w = [x for x in w if x[0] != "EV"] + [("EV", ev_out[m])]
                else:
                    w = [x for x in w if x[0] != "GP"] + [("GP", gp_out[m])]
            st = dma("sync", out_d.rearrange("c p m -> p c m")[:, g * 2:(g + 1) * 2, :],
                     PA[:, g * 2:(g + 1) * 2, :], "ST", waits=w)
            st_tot = st["ST"]
        p.add("sync", lambda s: None, waits=[("ST", st_tot)])

        p.emit_all(block, sems)
    return nc


_PERM_COLS = None


def _perm_cols():
    global _PERM_COLS
    if _PERM_COLS is None:
        idx = np.empty((IC, 128), dtype=np.int64)
        for c in range(IC):
            for pp in range(128):
                idx[c, pp] = 4 * (128 * (c // 4) + pp) + (c % 4)
        _PERM_COLS = idx
    return _PERM_COLS


def _prep_shared(w_qkv, w_out, w_gate, w_up, w_down, decay_param):
    w4 = np.ascontiguousarray(
        w_qkv.reshape(KC, 128, 3, H, 128).transpose(3, 2, 0, 1, 4))
    wo = np.ascontiguousarray(
        w_out.reshape(KC, 128, KC, 128).transpose(2, 0, 1, 3))
    idx = _perm_cols()
    wg = np.ascontiguousarray(
        w_gate[:, idx.reshape(-1)].reshape(KC, 128, IC, 128).transpose(2, 0, 1, 3))
    wu = np.ascontiguousarray(
        w_up[:, idx.reshape(-1)].reshape(KC, 128, IC, 128).transpose(2, 0, 1, 3))
    wd = np.ascontiguousarray(
        w_down[idx.reshape(-1), :].reshape(IC, 128, KC, 128).transpose(2, 0, 1, 3))
    dpbc = np.ascontiguousarray(np.tile(np.asarray(decay_param, np.float32)[None, :],
                                        (128, 1)))
    dp16 = np.ascontiguousarray(np.asarray(decay_param, np.float32).reshape(16, 1))
    iota = np.ascontiguousarray(
        np.tile(np.arange(1, J + 1, dtype=np.float32)[None, :], (128, 1)))
    ehc = np.zeros((H, 128, H), dtype=ml_dtypes.bfloat16)
    for h in range(H):
        ehc[h, :, h] = 1
    ohr = np.zeros((H, H, 128), dtype=ml_dtypes.bfloat16)
    for h in range(H):
        ohr[h, h, :] = 1
    return dict(wqkv=w4, wo=wo, wg=wg, wu=wu, wd=wd, dpbc=dpbc, dp16=dp16,
                iota=iota, ehc=ehc, ohr=ohr)


def kernel(L0_attn_x, L0_attn_w_qkv, L0_attn_w_out, L0_attn_decay_param,
           L0_ffn_w_gate, L0_ffn_w_up, L0_ffn_w_down):
    x = np.asarray(L0_attn_x, np.float32)
    shared = _prep_shared(np.asarray(L0_attn_w_qkv, np.float32),
                          np.asarray(L0_attn_w_out, np.float32),
                          np.asarray(L0_ffn_w_gate, np.float32),
                          np.asarray(L0_ffn_w_up, np.float32),
                          np.asarray(L0_ffn_w_down, np.float32),
                          L0_attn_decay_param)
    in_maps = []
    for c in range(8):
        b, half = c // 2, c % 2
        xs = np.ascontiguousarray(
            x[b, half * J:(half + 1) * J, :].T).reshape(KC, 128, J)
        flag = np.full((128, 1), float(half), np.float32)
        m = dict(shared)
        m["x"] = xs
        m["flag"] = flag
        in_maps.append(m)

    nc = build("full")
    res = run_bass_kernel_spmd(nc, in_maps, core_ids=list(range(8)))
    out = np.empty((4, 2048, 2048), dtype=np.float32)
    for c in range(8):
        b, half = c // 2, c % 2
        r = np.asarray(res.results[c]["out"])
        out[b, half * J:(half + 1) * J, :] = r.reshape(2048, J).T
    return out


# version 17
# speedup vs baseline: 1.0050x; 1.0050x over previous
import numpy as np
import ml_dtypes
from contextlib import ExitStack

import concourse.bass as bass
import concourse.mybir as mybir
from concourse.bass_utils import run_bass_kernel_spmd

F32 = mybir.dt.float32
BF16 = mybir.dt.bfloat16
AF = mybir.ActivationFunctionType
ALU = mybir.AluOpType

H = 16
D = 128
KC = 16
IC = 64
J = 1024
TH = 512
EPS_DEN = 1e-6
EPS_LN = 1e-5


class Prog:

    ENGS = ("tensor", "vector", "scalar", "sync", "gpsimd")
    ESEM = {"tensor": "MM", "vector": "EV", "scalar": "AC", "gpsimd": "GP"}

    def __init__(self, simsafe=False):
        self.seq = {e: [] for e in self.ENGS}
        self.cnt = {}
        self.simsafe = simsafe

    def val(self, sem):
        return self.cnt.get(sem, 0)

    def add(self, eng, emit, waits=(), incs=()):
        waits = list(waits)
        incs = list(incs)
        if self.simsafe and eng in self.ESEM:
            es = self.ESEM[eng]
            waits.append((es, self.val(es)))
            if not incs:
                incs.append((es, 1))
        self.seq[eng].append((emit, waits, incs))
        for s, a in incs:
            self.cnt[s] = self.val(s) + a
        return {s: self.val(s) for s, _ in incs}

    def emit_all(self, block, sems):
        def mk(eng_name):
            def body(eng):
                for emit, waits, incs in self.seq[eng_name]:
                    for s, v in waits:
                        if v > 0:
                            eng.wait_ge(sems[s], int(v))
                    inst = emit(eng)
                    for s, a in incs:
                        if inst is None:
                            eng.nop().then_inc(sems[s], a)
                        else:
                            inst = inst.then_inc(sems[s], a)
            return body
        block.tensor(mk("tensor"))
        block.vector(mk("vector"))
        block.scalar(mk("scalar"))
        block.sync(mk("sync"))
        block.gpsimd(mk("gpsimd"))


def build(stage="full", ncores=8, simsafe=False, detect_races=False):
    nc = bass.Bass("TRN2", target_bir_lowering=False, detect_race_conditions=detect_races)

    x_d = nc.dram_tensor("x", [KC, 128, J], F32, kind="ExternalInput").ap()
    wqkv_d = nc.dram_tensor("wqkv", [H, 3, KC, 128, 128], F32, kind="ExternalInput").ap()
    wo_d = nc.dram_tensor("wo", [KC, KC, 128, 128], F32, kind="ExternalInput").ap()
    wg_d = nc.dram_tensor("wg", [IC, KC, 128, 128], F32, kind="ExternalInput").ap()
    wu_d = nc.dram_tensor("wu", [IC, KC, 128, 128], F32, kind="ExternalInput").ap()
    wd_d = nc.dram_tensor("wd", [KC, IC, 128, 128], F32, kind="ExternalInput").ap()
    dpbc_d = nc.dram_tensor("dpbc", [128, H], F32, kind="ExternalInput").ap()
    dp16_d = nc.dram_tensor("dp16", [16, 1], F32, kind="ExternalInput").ap()
    flag_d = nc.dram_tensor("flag", [128, 1], F32, kind="ExternalInput").ap()
    iota_d = nc.dram_tensor("iota", [128, J], F32, kind="ExternalInput").ap()
    ehc_d = nc.dram_tensor("ehc", [H, 128, H], BF16, kind="ExternalInput").ap()
    ohr_d = nc.dram_tensor("ohr", [H, H, 128], BF16, kind="ExternalInput").ap()
    out_d = nc.dram_tensor("out", [KC, 128, J], F32, kind="ExternalOutput").ap()

    _bat = [(0, 16), (16, 14), (30, 2)]
    cc_ins = [nc.dram_tensor(f"cc_in{i}", [128, n], F32, kind="Internal")
              for i, (c0, n) in enumerate(_bat)]
    cc_outs = [nc.dram_tensor(f"cc_out{i}", [2, 128, n], F32, kind="Internal")
               for i, (c0, n) in enumerate(_bat)]

    p = Prog(simsafe=simsafe)
    SEM_NAMES = ["LC", "LX0", "LX1", "LX2", "LX3", "LX4", "LX5", "LX6", "LX7", "LKV", "LWA", "LWB", "LW0", "LW1", "LW2", "LW3", "LRA", "LRB", "CD0", "CD1", "CD2", "CR0", "CR1", "CR2", "CS", "MM", "EV", "AC", "GP", "ST"]

    with ExitStack() as ctx:
        e = ctx.enter_context
        PA = e(nc.sbuf_tensor("PA", [128, KC, J], F32))
        PB = e(nc.sbuf_tensor("PB", [128, KC, J], BF16))
        PC = e(nc.sbuf_tensor("PC", [128, KC, J], BF16))
        TF = e(nc.sbuf_tensor("TF", [128, 3, J], F32))
        DS = e(nc.sbuf_tensor("DS", [128, 2, J], F32))
        T4 = e(nc.sbuf_tensor("T4", [128, 2, J], BF16))
        WQKV = e(nc.sbuf_tensor("WQKV", [128, 2, 3, KC, 128], BF16))
        WD2 = e(nc.sbuf_tensor("WD2", [128, 2, KC, 128], BF16))
        XR = e(nc.sbuf_tensor("XR", [128, 2, J], F32))
        IOTA = e(nc.sbuf_tensor("IOTA", [128, J], F32))
        R16 = e(nc.sbuf_tensor("R16", [16, J], F32))
        R16B = e(nc.sbuf_tensor("R16B", [16, J], BF16))
        SI = e(nc.sbuf_tensor("SI", [128, 4 * H], F32))
        SEND = SI[:, 0:2 * H]
        INIT = SI[:, 2 * H:4 * H]
        INITB = e(nc.sbuf_tensor("INITB", [128, H], BF16))
        OH2 = e(nc.sbuf_tensor("OH2", [128, H, H], BF16))
        DL16 = e(nc.sbuf_tensor("DL16", [16, 2], F32))
        DEC16 = DL16[:, 0:1]
        LND16 = DL16[:, 1:2]
        EH = e(nc.sbuf_tensor("EH", [128, H, H], BF16))
        OHR = e(nc.sbuf_tensor("OHR", [16, H, 128], BF16))
        DPB = e(nc.sbuf_tensor("DPB", [128, H], F32))
        DEC = DPB
        LND = e(nc.sbuf_tensor("LND", [128, H], F32))
        FLAG = e(nc.sbuf_tensor("FLAG", [128, 1], F32))
        ONE_C = e(nc.sbuf_tensor("ONE_C", [128, 1], BF16))
        ONE_R = e(nc.sbuf_tensor("ONE_R", [1, 128], BF16))
        CONSTS = e(nc.sbuf_tensor("CONSTS", [128, 3], F32))
        EPSL = CONSTS[:, 0:1]
        NEGH = CONSTS[:, 1:2]
        NEG1 = CONSTS[:, 2:3]

        TFb = TF[:, :, :].bitcast(BF16)
        DSb = DS[:, :, :].bitcast(BF16)
        XRb = XR[:, :, :].bitcast(BF16)
        T5 = [XRb[:, 0, 0:J], XRb[:, 0, J:2 * J]]

        def bf_scratch(i):
            if i < 6:
                return TFb[:, i // 2, (i % 2) * J:(i % 2) * J + J]
            if i < 10:
                j = i - 6
                return DSb[:, j // 2, (j % 2) * J:(j % 2) * J + J]
            return T4[:, i - 10, :]

        P0 = e(nc.psum_tensor("P0", [128, J], F32))
        P1 = e(nc.psum_tensor("P1", [128, J], F32))
        P2 = e(nc.psum_tensor("P2", [128, J], F32))
        P3 = e(nc.psum_tensor("P3", [128, J], F32))
        PS = [P0, P1, P2, P3]

        sems = {}
        for s in SEM_NAMES:
            sems[s] = e(nc.semaphore(s))
        _sems = sems
        block = e(nc.Block())

        def mm_group(psum, stat_fn, mov_fn, nk, waits, inc_sem="MM",
                     start=True, stop=True):
            def emit(t):
                inst = None
                for k in range(nk):
                    st = stat_fn(k)
                    for th in range(2):
                        inst = t.matmul(
                            psum[:, th * TH:(th + 1) * TH],
                            st, mov_fn(k, th),
                            start=(start and k == 0), stop=(stop and k == nk - 1))
                return inst
            return p.add("tensor", emit, waits=waits, incs=[(inc_sem, 1)])

        def vop(emit, waits=(), inc=True):
            return p.add("vector", emit, waits=waits,
                         incs=[("EV", 1)] if inc else [])

        def aop(emit, waits=(), inc=True):
            return p.add("scalar", emit, waits=waits,
                         incs=[("AC", 1)] if inc else [])

        def dma(queue, out, in_, sem, waits=()):
            return p.add(queue, lambda g: g.dma_start(out=out, in_=in_),
                         waits=waits, incs=[(sem, 16)])

        dma("sync", DPB[:, :], dpbc_d, "LC")
        dma("sync", DEC16, dp16_d, "LC")
        dma("sync", FLAG[:, :], flag_d, "LC")
        dma("sync", IOTA[:, :], iota_d, "LC")
        dma("sync", EH[:, :, :], ehc_d.rearrange("h p c -> p h c"), "LC")
        dma("sync", OHR[:, :, :], ohr_d.rearrange("h p c -> p h c"), "LC")
        lr_pre = p.val("LC")

        dma("gpsimd", WQKV[:, 0, 0, :, :], wqkv_d[0, 0].rearrange("k p m -> p k m"), "LWA")
        for i in range(8):
            dma("gpsimd", PB[:, 2 * i:2 * i + 2, :],
                x_d.rearrange("c p m -> p c m")[:, 2 * i:2 * i + 2, :], f"LX{i}")
        dma("gpsimd", WQKV[:, 0, 1:3, :, :],
            wqkv_d[0, 1:3].rearrange("j k p m -> p j k m"), "LKV")

        vop(lambda v: v.memset(ONE_C[:, :], 1.0), inc=False)
        vop(lambda v: v.memset(ONE_R[:, :], 1.0), inc=False)
        vop(lambda v: v.memset(EPSL, EPS_LN), inc=False)
        vop(lambda v: v.memset(NEGH, -0.5), inc=False)
        vop(lambda v: v.memset(NEG1, -1.0), inc=False)
        ev_oh2z = vop(lambda v: v.memset(OH2[:, :, :], 0.0))["EV"]
        ac_sig = aop(lambda s: s.activation(DEC[:, :], DPB[:, :], AF.Sigmoid),
                     waits=[("LC", lr_pre)])["AC"]
        ac_dec = aop(lambda s: s.activation(LND[:, :], DEC[:, :], AF.Ln),
                     waits=[("AC", ac_sig)])["AC"]
        aop(lambda s: s.activation(DEC16, DEC16, AF.Sigmoid))
        aop(lambda s: s.activation(LND16, DEC16, AF.Ln))
        ac_dp16 = aop(lambda s: s.activation(R16[:, :], IOTA[0:16, :], AF.Exp,
                                             scale=LND16))["AC"]

        mm_q = [0] * H; mm_k = [0] * H; mm_v = [0] * H
        mm_den = [0] * H
        ev_qphi = [0] * H; ev_kphi = [0] * H; ev_xkv = [0] * H
        ev_p = [0] * H; ev_state = [0] * H
        lw_head = [0] * H
        mm_last_of_head = [0] * H

        ac_dpow = [0] * H
        ev_rq = [0] * H; ev_kvc = [0] * H; ev_oh = [0] * H
        mm_denc = [0] * H
        ev_init = [0, 0, 0]
        ev_initb = [0, 0, 0]

        def _bi(h):
            return 0 if h < 8 else (1 if h < 15 else 2)

        def emit_a3a(h):
            hb = h % 2
            half = _bi(h)
            w = [("AC", ac_dec), ("LC", lr_pre)]
            if h >= 2:
                w.append(("EV", ev_kvc[h - 2]))
            ac_dpow[h] = aop(lambda s, h=h, hb=hb: s.activation(
                DS[:, hb, :], IOTA[:, :], AF.Exp, scale=LND[:, h:h + 1]), waits=w)["AC"]
            if h < 8:
                ev_kvc[h] = vop(lambda v, h=h, hb=hb: v.scalar_tensor_tensor(
                    PA[:, h, :], DS[:, hb, :], INIT[:, 2 * h:2 * h + 1],
                    PA[:, h, :], ALU.mult, ALU.add),
                    waits=[("AC", ac_dpow[h]), ("EV", ev_init[half])])["EV"]
            else:
                sl = 2 if h % 2 == 0 else 0
                w_kt = [("AC", ac_dpow[h]), ("EV", ev_init[half])]
                if h >= 10:
                    w_kt.append(("EV", ev_kvc[h - 2]))
                ac_kt = aop(lambda s, h=h, hb=hb, sl=sl: s.activation(
                    TF[:, sl, :], DS[:, hb, :], AF.Copy,
                    scale=INIT[:, 2 * h:2 * h + 1]), waits=w_kt)["AC"]
                ev_kvc[h] = vop(lambda v, h=h, sl=sl: v.tensor_tensor(
                    PA[:, h, :], PA[:, h, :], TF[:, sl, :], ALU.add),
                    waits=[("AC", ac_kt)])["EV"]

        ev_oh2 = [0, 0, 0]

        def emit_oh2(batch):
            c0, n = _bat[batch]
            for h in range(c0 // 2, (c0 + n) // 2):
                last = (h == (c0 + n) // 2 - 1)
                vop(lambda v, h=h: v.tensor_copy(
                    OH2[:, h, h:h + 1], INITB[:, h:h + 1]),
                    waits=[("EV", ev_initb[batch]), ("EV", ev_oh2z)], inc=last)
            ev_oh2[batch] = p.val("EV")

        def emit_denc(h):
            def denc_emit(t, h=h):
                inst = None
                for th in range(2):
                    inst = t.matmul(P2[0:16, th * TH:(th + 1) * TH],
                                    OH2[:, h, :], PC[:, h, th * TH:(th + 1) * TH],
                                    start=(h == 0), stop=(h == H - 1))
                return inst
            mm_denc[h] = p.add("tensor", denc_emit,
                               waits=[("EV", ev_oh2[_bi(h)])],
                               incs=[("MM", 1)])["MM"]

        def emit_cc(batch, w_state):
            c0, n = _bat[batch]
            dma("sync", cc_ins[batch].ap(), SEND[:, c0:c0 + n], f"CD{batch}",
                waits=[("EV", w_state)])
            p.add("gpsimd", lambda g, batch=batch: g.collective_compute(
                "AllGather", ALU.bypass,
                replica_groups=[[i, i + 1] for i in range(0, ncores, 2)],
                ins=[cc_ins[batch].ap()], outs=[cc_outs[batch].ap()]),
                waits=[(f"CD{batch}", 16)], incs=[("CS", 1)])

        def emit_cc_recv(batch):
            c0, n = _bat[batch]
            dma("sync", INIT[:, c0:c0 + n], cc_outs[batch].ap()[0, :, :],
                f"CR{batch}", waits=[("CS", batch + 1)])

        def emit_init(batch):
            c0, n = _bat[batch]
            ev_init[batch] = vop(lambda v, c0=c0, n=n: v.tensor_scalar(
                INIT[:, c0:c0 + n], INIT[:, c0:c0 + n],
                FLAG[:, :], None, ALU.mult),
                waits=[(f"CR{batch}", 16), ("LC", lr_pre)])["EV"]
            ev_initb[batch] = vop(lambda v, c0=c0, n=n: v.tensor_copy(
                INITB[:, c0 // 2:(c0 + n) // 2],
                INIT[:, c0 + 1:c0 + n:2]))["EV"]

        for h in range(H):
            hb = h % 2
            lwsem = "LWA" if hb == 0 else "LWB"
            if h > 0 and h != 15:
                waits = [("MM", mm_last_of_head[h - 2])] if h >= 2 else []
                dma("gpsimd", WQKV[:, hb, :, :, :],
                    wqkv_d[h].rearrange("j k p m -> p j k m"), lwsem, waits=waits)
            lw_head[h] = p.val(lwsem)

            stat = lambda j, hb=hb: (lambda k: WQKV[:, hb, j, k, :])
            mov = lambda k, th: PB[:, k, th * TH:(th + 1) * TH]

            if h == 0:
                def q0_emit(t, sems_local=None):
                    inst = None
                    for k in range(KC):
                        if k % 2 == 0:
                            t.wait_ge(_sems[f"LX{k // 2}"], 16)
                        st0 = WQKV[:, 0, 0, k, :]
                        for th in range(2):
                            inst = t.matmul(P0[:, th * TH:(th + 1) * TH], st0,
                                            mov(k, th),
                                            start=(k == 0), stop=(k == KC - 1))
                    return inst
                mm_q[h] = p.add("tensor", q0_emit, waits=[("LWA", 16)],
                                incs=[("MM", 1)])["MM"]
            else:
                w_q = [("LWA" if hb == 0 else "LWB", lw_head[h])]
                w_q.append(("EV", ev_qphi[h - 1]))
                mm_q[h] = mm_group(P0, stat(0), mov, KC, w_q)["MM"]
            w_k = [("EV", ev_kphi[h - 1])] if h >= 1 else [("LKV", 16)]
            mm_k[h] = mm_group(P1, stat(1), mov, KC, w_k)["MM"]
            w_v = [("EV", ev_xkv[h - 1])] if h >= 1 else []
            mm_v[h] = mm_group(P2, stat(2), mov, KC, w_v)["MM"]

            ev1 = vop(lambda v: v.tensor_scalar(TF[:, 0, :], P0[:, :], 0.0, None, ALU.min),
                      waits=[("MM", mm_q[h])])["EV"]
            ac1 = aop(lambda s: s.activation(TF[:, 1, :], TF[:, 0, :], AF.Exp),
                      waits=[("EV", ev1)])["AC"]
            ev_qphi[h] = vop(
                lambda v, h=h: v.scalar_tensor_tensor(
                    PC[:, h, :], P0[:, :], 0.0, TF[:, 1, :], ALU.max, ALU.add),
                waits=[("AC", ac1)])["EV"]
            ev2 = vop(lambda v: v.tensor_scalar(TF[:, 2, :], P1[:, :], 0.0, None, ALU.min),
                      waits=[("MM", mm_k[h])])["EV"]
            ac2 = aop(lambda s: s.activation(TF[:, 1, :], TF[:, 2, :], AF.Exp),
                      waits=[("EV", ev2)])["AC"]
            ev_kphi[h] = vop(
                lambda v: v.scalar_tensor_tensor(
                    TF[:, 0, :], P1[:, :], 0.0, TF[:, 1, :], ALU.max, ALU.add),
                waits=[("AC", ac2)])["EV"]
            ev_xkv[h] = vop(
                lambda v: v.tensor_tensor(TF[:, 2, :], TF[:, 0, :], P2[:, :], ALU.mult),
                waits=[("MM", mm_v[h])])["EV"]
            vop(lambda v, h=h: v.tensor_tensor_scan(
                PA[:, h, :], DEC[:, h:h + 1].broadcast_to([128, J]), TF[:, 2, :],
                0.0, ALU.mult, ALU.add),
                waits=[("AC", ac_dec)])
            vop(lambda v, h=h: v.tensor_tensor_scan(
                TF[:, 1, :], DEC[:, h:h + 1].broadcast_to([128, J]), TF[:, 0, :],
                0.0, ALU.mult, ALU.add))
            w_p = [("MM", mm_den[h - 2])] if h >= 2 else []
            ev_p[h] = vop(lambda v, h=h, hb=hb: v.tensor_tensor(
                T4[:, hb, :], PC[:, h, :], TF[:, 1, :], ALU.mult), waits=w_p)["EV"]
            vop(lambda v, h=h: v.tensor_copy(SEND[:, 2 * h:2 * h + 1],
                                             PA[:, h, J - 1:J]), inc=False)
            ev_state[h] = vop(lambda v, h=h: v.tensor_copy(
                SEND[:, 2 * h + 1:2 * h + 2], TF[:, 1, J - 1:J]))["EV"]
            def mk_den(h):
                def den_emit(t, h=h):
                    inst = None
                    for th in range(2):
                        inst = t.matmul(P3[0:16, th * TH:(th + 1) * TH],
                                        EH[:, h, :], T4[:, h % 2, th * TH:(th + 1) * TH],
                                        start=(h == 0), stop=(h == H - 1))
                    return inst
                return den_emit
            if h >= 1:
                mm_den[h - 1] = p.add("tensor", mk_den(h - 1),
                                      waits=[("EV", ev_p[h - 1])], incs=[("MM", 1)])["MM"]
            if h == H - 1:
                mm_den[h] = p.add("tensor", mk_den(h),
                                  waits=[("EV", ev_p[h])], incs=[("MM", 1)])["MM"]
            mm_last_of_head[h] = p.val("MM")

            if h == 8:
                emit_cc(0, ev_state[7])
            elif h == 10:
                emit_cc_recv(0)
            elif h == 11:
                emit_init(0)
                emit_oh2(0)
            elif h == 14:
                dma("gpsimd", WQKV[:, 1, :, :, :],
                    wqkv_d[15].rearrange("j k p m -> p j k m"), "LWB",
                    waits=[("MM", mm_last_of_head[13])])
                lw_head[15] = p.val("LWB")
                emit_cc(1, ev_state[14])
                emit_a3a(4)
                emit_a3a(5)
            elif h >= 12:
                j0 = 2 * (h - 12) if h < 14 else 6
                emit_a3a(j0)
                emit_a3a(j0 + 1)

        emit_cc(2, ev_state[15])

        mm_wo = [0] * KC
        ev_r = [0] * KC
        lw_wo = [0] * KC
        def wo_buf(m):
            r = m % 4
            return WQKV[:, r % 2, r // 2, :, :]
        for m in range(4):
            dma("gpsimd", wo_buf(m),
                wo_d[m].rearrange("k p m -> p k m"), f"LW{m % 4}",
                waits=[("MM", mm_last_of_head[H - 1])])
            lw_wo[m] = p.val(f"LW{m % 4}")

        for j in range(0, 8):
            emit_denc(j)
        emit_cc_recv(1)
        emit_init(1)
        emit_oh2(1)
        for j in range(8, 15):
            emit_a3a(j)
        for j in range(8, 15):
            emit_denc(j)
        emit_cc_recv(2)
        emit_init(2)
        emit_oh2(2)
        emit_a3a(15)
        emit_denc(15)

        DS16 = DS[0:16, 0, :]
        vop(lambda v: v.tensor_tensor(DS16, R16[:, :], P2[0:16, :], ALU.mult),
            waits=[("MM", mm_denc[H - 1]), ("AC", ac_dp16)], inc=False)
        vop(lambda v: v.tensor_tensor(DS16, DS16, P3[0:16, :], ALU.add), inc=False)
        ev_dclip = vop(lambda v: v.tensor_scalar(DS16, DS16, EPS_DEN, None,
                                                 ALU.max))["EV"]
        aop(lambda s: s.activation(DS16, DS16, AF.Ln), waits=[("EV", ev_dclip)])
        ac_rden = aop(lambda s: s.activation(R16B[:, :], DS16, AF.Exp,
                                             scale=NEG1[0:16]))["AC"]
        ev_rden = None

        mm_rbc = [0] * H
        ev_oa = [0] * H
        gp_t2 = [0] * H
        for h in range(H):
            hb = h % 2
            Pr = PS[hb]
            w = [("AC", ac_rden)]
            if h >= 2:
                w.append(("EV", ev_oa[h - 2]))
            def rbc_emit(t, h=h, Pr=Pr):
                inst = None
                for th in range(2):
                    inst = t.matmul(Pr[:, th * TH:(th + 1) * TH],
                                    OHR[:, h, :], R16B[:, th * TH:(th + 1) * TH],
                                    start=True, stop=True)
                return inst
            mm_rbc[h] = p.add("tensor", rbc_emit, waits=w, incs=[("MM", 1)])["MM"]
            if h % 2 == 0:
                gp_t2[h] = p.add("gpsimd", lambda g, h=h: g.tensor_tensor(
                    PB[:, h, :], PA[:, h, :], PC[:, h, :], ALU.mult),
                    waits=[("EV", ev_kvc[h]),
                           ("MM", mm_last_of_head[H - 1])],
                    incs=[("GP", 1)])["GP"]
                ev_oa[h] = vop(lambda v, h=h, Pr=Pr: v.tensor_tensor(
                    PB[:, h, :], PB[:, h, :], Pr[:, :], ALU.mult),
                    waits=[("MM", mm_rbc[h]), ("GP", gp_t2[h])])["EV"]
            else:
                vop(lambda v, h=h: v.tensor_tensor(
                    PB[:, h, :], PA[:, h, :], PC[:, h, :], ALU.mult),
                    waits=[("EV", ev_kvc[h]),
                           ("MM", mm_last_of_head[H - 1])], inc=False)
                ev_oa[h] = vop(lambda v, h=h, Pr=Pr: v.tensor_tensor(
                    PB[:, h, :], PB[:, h, :], Pr[:, :], ALU.mult),
                    waits=[("MM", mm_rbc[h])])["EV"]

        lr_x = [0] * KC
        mm_ln1 = [0] * KC
        ac_sq = [0] * KC
        for m in range(KC):
            lwsem = f"LW{m % 4}"
            if m >= 4:
                dma("gpsimd", wo_buf(m),
                    wo_d[m].rearrange("k p m -> p k m"), lwsem,
                    waits=[("MM", mm_wo[m - 4])])
                lw_wo[m] = p.val(lwsem)
            lrsem = "LRA" if m % 2 == 0 else "LRB"
            if m >= 2:
                w = [("EV", ev_r[m - 2])]
            else:
                w = [("MM", mm_denc[H - 1])]
            dma("sync", XR[:, m % 2, :], x_d[m], lrsem, waits=w)
            lr_x[m] = p.val(lrsem)

            Pw = PS[m % 2]
            w = [(lwsem, lw_wo[m]), ("EV", ev_oa[H - 1])]
            if m >= 2:
                w.append(("EV", ev_r[m - 2]))
            mm_wo[m] = mm_group(
                Pw, lambda k, m=m: wo_buf(m)[:, k, :],
                lambda k, th: PB[:, k, th * TH:(th + 1) * TH], KC, w)["MM"]
            vop(lambda v, m=m, Pw=Pw: v.tensor_tensor(
                PA[:, m, :], Pw[:, :], XR[:, m % 2, :], ALU.add),
                waits=[("MM", mm_wo[m]), ("LRA" if m % 2 == 0 else "LRB", lr_x[m])], inc=False)
            ev_r[m] = vop(lambda v, m=m: v.tensor_copy(PC[:, m, :], PA[:, m, :]))["EV"]
            w = [("EV", ev_r[m])]
            if m >= 2:
                w.append(("MM", mm_ln1[m - 2]))
            ac_sq[m] = aop(lambda s, m=m: s.activation(
                T4[:, m % 2, :], PA[:, m, :], AF.Square), waits=w)["AC"]
            def mk_ln1(m):
                def ln1_emit(t, m=m):
                    inst = None
                    for th in range(2):
                        t.matmul(P2[0:1, th * TH:(th + 1) * TH], ONE_C[:, :],
                                 PC[:, m, th * TH:(th + 1) * TH],
                                 start=(m == 0), stop=(m == KC - 1))
                        inst = t.matmul(P3[0:1, th * TH:(th + 1) * TH], ONE_C[:, :],
                                        T4[:, m % 2, th * TH:(th + 1) * TH],
                                        start=(m == 0), stop=(m == KC - 1))
                    return inst
                return ln1_emit
            if m >= 1:
                mm_ln1[m - 1] = p.add(
                    "tensor", mk_ln1(m - 1),
                    waits=[("EV", ev_r[m - 1]), ("AC", ac_sq[m - 1])],
                    incs=[("MM", 1)])["MM"]
            if m == KC - 1:
                mm_ln1[m] = p.add(
                    "tensor", mk_ln1(m),
                    waits=[("EV", ev_r[m]), ("AC", ac_sq[m])],
                    incs=[("MM", 1)])["MM"]

        def ln_rows_and_bcast(mm_wait, Psum_s, Psum_q, tag):
            mean = DS[0:1, 0, :]
            ex2 = DS[0:1, 1, :]
            rstd = R16[0:1, :]
            a_row = T4[0:1, 0, :]
            b_row = T4[0:1, 1, :]
            vop(lambda v: v.tensor_scalar(mean, Psum_s[0:1, :],
                                          1.0 / 2048, None, ALU.mult),
                waits=[("MM", mm_wait)], inc=False)
            vop(lambda v: v.tensor_scalar(ex2, Psum_q[0:1, :],
                                          1.0 / 2048, None, ALU.mult), inc=False)
            vop(lambda v: v.tensor_tensor(rstd, mean, mean, ALU.mult), inc=False)
            ev_v = vop(lambda v: v.tensor_tensor(ex2, ex2, rstd, ALU.subtract))["EV"]
            aop(lambda s: s.activation(rstd, ex2, AF.Ln, bias=EPSL[0:1]),
                waits=[("EV", ev_v)])
            aop(lambda s: s.activation(a_row, rstd, AF.Exp,
                                       scale=NEGH[0:1]))
            ac_rs = aop(lambda s: s.activation(b_row, mean, AF.Copy))["AC"]
            def bc_emit(t):
                inst = None
                for th in range(2):
                    t.matmul(Psum_s[:, th * TH:(th + 1) * TH], ONE_R[:, :],
                             a_row[:, th * TH:(th + 1) * TH], start=True, stop=True)
                    inst = t.matmul(Psum_q[:, th * TH:(th + 1) * TH], ONE_R[:, :],
                                    b_row[:, th * TH:(th + 1) * TH],
                                    start=True, stop=True)
                return inst
            return p.add("tensor", bc_emit, waits=[("AC", ac_rs)],
                         incs=[("MM", 1)])["MM"]

        mm_bc1 = ln_rows_and_bcast(mm_ln1[KC - 1], P2, P3, "ln1")

        ev_x2 = [0] * KC
        ac_x2h = [0] * KC
        for m in range(KC):
            vop(lambda v, m=m: v.tensor_tensor(TF[:, 0, :], PA[:, m, :], P3[:, :],
                                               ALU.subtract),
                waits=[("MM", mm_bc1)], inc=False)
            ev_x2[m] = vop(lambda v, m=m: v.tensor_tensor(
                PA[:, m, :], TF[:, 0, :], P2[:, :], ALU.mult))["EV"]
            ac_x2h[m] = aop(lambda s, m=m: s.activation(
                PC[:, m, :], PA[:, m, :], AF.Copy), waits=[("EV", ev_x2[m])])["AC"]

        if stage == "x2":
            st = dma("sync", out_d.rearrange("c p m -> p c m"), PA[:, :, :], "ST",
                     waits=[("EV", ev_x2[KC - 1]), ("AC", ac_x2h[KC - 1])])
            p.add("sync", lambda s: None, waits=[("ST", st["ST"])])
            p.emit_all(block, sems)
            return nc

        mm_gu = [0] * IC
        ev_h = [0] * IC
        ev_mask = [0] * (IC // 4)
        mm_dn = np.zeros((4, KC), dtype=int)
        ev_dn = np.zeros((4, KC), dtype=int)
        lw_gu = [0] * IC
        lw_dn = np.zeros((4, KC), dtype=int)
        ac_silu = [0] * IC
        ac_abs = [0] * IC

        WG = lambda b: WQKV[:, b, 0, :, :]
        WU = lambda b: WQKV[:, b, 1, :, :]

        def WD(r):
            return WQKV[:, r, 2, :, :] if r < 2 else WD2[:, r - 2, :, :]

        for qt in range(4):
            for lc in range(16):
                c = qt * 16 + lc
                cb = c % 2
                lwsem = "LWA" if cb == 0 else "LWB"
                if c >= 2:
                    w = [("MM", mm_gu[c - 2])]
                elif c == 0:
                    w = [("MM", mm_wo[12])]
                else:
                    w = [("MM", mm_wo[13])]
                dma("gpsimd", WG(cb), wg_d[c].rearrange("k p m -> p k m"), lwsem, waits=w)
                wu_w = []
                if c == 0:
                    wu_w = [("MM", mm_wo[14])]
                elif c == 1:
                    wu_w = [("MM", mm_wo[15])]
                dma("gpsimd", WU(cb), wu_d[c].rearrange("k p m -> p k m"), lwsem,
                    waits=wu_w)
                lw_gu[c] = p.val(lwsem)
                m_wd = lc
                if m_wd < 4:
                    if qt == 0:
                        w2 = [("MM", mm_wo[12 + m_wd])]
                    else:
                        w2 = [("MM", mm_dn[qt - 1][KC - 4 + m_wd])]
                    dma("gpsimd", WD(m_wd), wd_d[m_wd, qt * 16:(qt + 1) * 16].rearrange(
                        "c p m -> p c m"), f"LW{m_wd}", waits=w2)
                    lw_dn[qt][m_wd] = p.val(f"LW{m_wd}")

                Pg, Pu = PS[cb * 2], PS[cb * 2 + 1]
                if c == 0:
                    def g0_emit(t, Pp, wsel):
                        inst = None
                        for k in range(KC):
                            t.wait_ge(_sems["AC"], ac_x2h[k])
                            stw = (WG if wsel == 0 else WU)(0)[:, k, :]
                            for th in range(2):
                                inst = t.matmul(Pp[:, th * TH:(th + 1) * TH], stw,
                                                PC[:, k, th * TH:(th + 1) * TH],
                                                start=(k == 0), stop=(k == KC - 1))
                        return inst
                    p.add("tensor", lambda t, Pg=Pg: g0_emit(t, Pg, 0),
                          waits=[(lwsem, lw_gu[c])], incs=[("MM", 1)])
                    mm_gu[c] = p.add("tensor", lambda t, Pu=Pu: g0_emit(t, Pu, 1),
                                     incs=[("MM", 1)])["MM"]
                else:
                    w = [(lwsem, lw_gu[c]), ("AC", ac_x2h[KC - 1])]
                    if c >= 2:
                        w.append(("EV", ev_h[c - 2]))
                    if qt > 0 and lc == 1:
                        w.append(("EV", ev_dn[qt - 1][KC - 1]))
                    if qt == 0 and lc == 1:
                        w.append(("EV", ev_x2[KC - 1]))
                    mm_group(Pg, lambda k, cb=cb: WG(cb)[:, k, :],
                             lambda k, th: PC[:, k, th * TH:(th + 1) * TH], KC, w)
                    mm_gu[c] = mm_group(Pu, lambda k, cb=cb: WU(cb)[:, k, :],
                                        lambda k, th: PC[:, k, th * TH:(th + 1) * TH],
                                        KC, [], start=True, stop=True)["MM"]
                w = [("MM", mm_gu[c])]
                if c >= 2:
                    w.append(("EV", ev_h[c - 2]))
                if lc % 4 == 0 and c > 0:
                    w.append(("EV", ev_mask[c // 4 - 1]))
                ac_silu[c] = aop(lambda s, cb=cb, Pg=Pg: s.activation(
                    DS[:, cb, :], Pg[:, :], AF.Silu), waits=w)["AC"]
                ev_h[c] = vop(lambda v, lc=lc, cb=cb, Pu=Pu: v.tensor_tensor(
                    PB[:, lc, :], DS[:, cb, :], Pu[:, :], ALU.mult),
                    waits=[("AC", ac_silu[c])])["EV"]
                ac_abs[c] = aop(lambda s, lc=lc: s.activation(
                    bf_scratch(lc % 4), PB[:, lc, :], AF.Abs),
                    waits=[("EV", ev_h[c])])["AC"]

                if lc % 4 == 1:
                    w = [("AC", ac_abs[c])]
                    vop(lambda v: v.tensor_tensor(bf_scratch(4), bf_scratch(0), bf_scratch(1), ALU.is_ge), waits=w, inc=False)
                if lc % 4 == 2:
                    w = [("AC", ac_abs[c])]
                    vop(lambda v: v.tensor_tensor(bf_scratch(5), bf_scratch(0), bf_scratch(2), ALU.is_ge), waits=w, inc=False)
                    vop(lambda v: v.tensor_tensor(bf_scratch(10), bf_scratch(1), bf_scratch(2), ALU.is_ge), inc=False)
                    vop(lambda v: v.tensor_tensor(bf_scratch(11), bf_scratch(4), bf_scratch(5), ALU.add), inc=False)
                    vop(lambda v: v.tensor_tensor(bf_scratch(5), bf_scratch(5), bf_scratch(10), ALU.add), inc=False)
                if lc % 4 == 3:
                    quad = c // 4
                    a = [bf_scratch(i) for i in range(4)]
                    c01, s2a, c12, w0a = bf_scratch(4), bf_scratch(5), bf_scratch(10), bf_scratch(11)
                    c03, c13, c23, tw = bf_scratch(6), bf_scratch(7), bf_scratch(8), bf_scratch(9)
                    l0, l1, l2, l3 = lc - 3, lc - 2, lc - 1, lc
                    w = [("AC", ac_abs[c])]
                    vop(lambda v, a=a, c03=c03: v.tensor_tensor(c03, a[0], a[3], ALU.is_ge), waits=w, inc=False)
                    vop(lambda v, a=a, c13=c13: v.tensor_tensor(c13, a[1], a[3], ALU.is_ge), inc=False)
                    vop(lambda v, a=a, c23=c23: v.tensor_tensor(c23, a[2], a[3], ALU.is_ge), inc=False)
                    vop(lambda v: v.tensor_tensor(tw, w0a, c03, ALU.add), inc=False)
                    vop(lambda v, a=a: v.tensor_scalar(a[0], tw, 1.5, None, ALU.is_ge), inc=False)
                    vop(lambda v: v.tensor_tensor(tw, c12, c13, ALU.add), inc=False)
                    vop(lambda v: v.tensor_tensor(tw, tw, c01, ALU.subtract), inc=False)
                    vop(lambda v, a=a: v.tensor_scalar(a[1], tw, 0.5, None, ALU.is_ge), inc=False)
                    vop(lambda v: v.tensor_tensor(tw, c23, s2a, ALU.subtract), inc=False)
                    vop(lambda v, a=a: v.tensor_scalar(a[2], tw, -0.5, None, ALU.is_ge), inc=False)
                    vop(lambda v: v.tensor_tensor(tw, c03, c13, ALU.add), inc=False)
                    vop(lambda v: v.tensor_tensor(tw, tw, c23, ALU.add), inc=False)
                    vop(lambda v, a=a: v.tensor_scalar(a[3], tw, 1.5, None, ALU.is_le), inc=False)
                    for i, li in enumerate((l0, l1, l2, l3)):
                        last = (i == 3)
                        vop(lambda v, li=li, i=i, a=a: v.tensor_tensor(
                            PB[:, li, :], PB[:, li, :], a[i], ALU.mult), inc=last)
                    ev_mask[quad] = p.val("EV")

            for m in range(KC):
                mb = m % 2
                mr = m % 4
                if m >= 4:
                    ldsem = f"LW{mr}"
                    dma("gpsimd", WD(mr), wd_d[m, qt * 16:(qt + 1) * 16].rearrange(
                        "c p m -> p c m"), ldsem, waits=[("MM", mm_dn[qt][m - 4])])
                    lw_dn[qt][m] = p.val(ldsem)
                ldsem = f"LW{mr}"

                Pd = PS[2 + mb]
                w = [(ldsem, lw_dn[qt][m]),
                     ("EV", ev_h[qt * 16 + 15])]
                if m >= 2:
                    w.append(("EV", ev_dn[qt][m - 2]))
                elif qt > 0:
                    w.append(("EV", ev_dn[qt - 1][KC - 1]))
                def dn_emit(t, mr=mr, Pd=Pd, qt=qt):
                    inst = None
                    for k in range(KC):
                        if k % 4 == 0:
                            t.wait_ge(_sems["EV"], int(ev_mask[qt * 4 + k // 4]))
                        stw = WD(mr)[:, k, :]
                        for th in range(2):
                            inst = t.matmul(Pd[:, th * TH:(th + 1) * TH], stw,
                                            PB[:, k, th * TH:(th + 1) * TH],
                                            start=(k == 0), stop=(k == KC - 1))
                    return inst
                mm_dn[qt][m] = p.add("tensor", dn_emit, waits=w,
                                     incs=[("MM", 1)])["MM"]
                ev_dn[qt][m] = vop(lambda v, m=m, Pd=Pd: v.tensor_tensor(
                    PA[:, m, :], PA[:, m, :], Pd[:, :], ALU.add),
                    waits=[("MM", mm_dn[qt][m])])["EV"]

        mm_ln2 = [0] * KC
        ev_r2h = [0] * KC
        ac_sq2 = [0] * KC
        for m in range(KC):
            ev_r2h[m] = vop(lambda v, m=m: v.tensor_copy(PC[:, m, :], PA[:, m, :]),
                            waits=[("EV", ev_dn[3][m])])["EV"]
            w = [("EV", ev_r2h[m])]
            if m >= 2:
                w.append(("MM", mm_ln2[m - 2]))
            ac_sq2[m] = aop(lambda s, m=m: s.activation(
                T4[:, m % 2, :], PA[:, m, :], AF.Square), waits=w)["AC"]
            def mk_ln2(m):
                def ln2_emit(t, m=m):
                    inst = None
                    for th in range(2):
                        t.matmul(P2[0:1, th * TH:(th + 1) * TH], ONE_C[:, :],
                                 PC[:, m, th * TH:(th + 1) * TH],
                                 start=(m == 0), stop=(m == KC - 1))
                        inst = t.matmul(P3[0:1, th * TH:(th + 1) * TH], ONE_C[:, :],
                                        T4[:, m % 2, th * TH:(th + 1) * TH],
                                        start=(m == 0), stop=(m == KC - 1))
                    return inst
                return ln2_emit
            if m >= 1:
                w = [("EV", ev_r2h[m - 1]), ("AC", ac_sq2[m - 1])]
                if m == 1:
                    w.append(("EV", ev_dn[3][KC - 1]))
                mm_ln2[m - 1] = p.add("tensor", mk_ln2(m - 1), waits=w,
                                      incs=[("MM", 1)])["MM"]
            if m == KC - 1:
                mm_ln2[m] = p.add("tensor", mk_ln2(m),
                                  waits=[("EV", ev_r2h[m]), ("AC", ac_sq2[m])],
                                  incs=[("MM", 1)])["MM"]

        mm_bc2 = ln_rows_and_bcast(mm_ln2[KC - 1], P2, P3, "ln2")

        vop(lambda v: v.tensor_copy(TF[:, 1, :], P2[:, :]),
            waits=[("MM", mm_bc2)], inc=False)
        ev_bce = vop(lambda v: v.tensor_copy(TF[:, 2, :], P3[:, :]))["EV"]
        ev_out = [0] * KC
        gp_out = [0] * KC
        st_tot = 0
        for g in range(8):
            for lm in range(2):
                m = g * 2 + lm
                if m < 11:
                    vop(lambda v, m=m: v.tensor_tensor(TF[:, 0, :], PA[:, m, :],
                                                       P3[:, :], ALU.subtract),
                        waits=[("MM", mm_bc2)], inc=False)
                    ev_out[m] = vop(lambda v, m=m: v.tensor_tensor(
                        PA[:, m, :], TF[:, 0, :], P2[:, :], ALU.mult))["EV"]
                else:
                    p.add("gpsimd", lambda g2, m=m: g2.tensor_tensor(
                        DS[:, 0, :], PA[:, m, :], TF[:, 2, :], ALU.subtract),
                        waits=[("EV", ev_bce)])
                    gp_out[m] = p.add("gpsimd", lambda g2, m=m: g2.tensor_tensor(
                        PA[:, m, :], DS[:, 0, :], TF[:, 1, :], ALU.mult),
                        incs=[("GP", 1)])["GP"]
            w = []
            for lm in range(2):
                m = g * 2 + lm
                if m < 11:
                    w = [x for x in w if x[0] != "EV"] + [("EV", ev_out[m])]
                else:
                    w = [x for x in w if x[0] != "GP"] + [("GP", gp_out[m])]
            st = dma("sync", out_d.rearrange("c p m -> p c m")[:, g * 2:(g + 1) * 2, :],
                     PA[:, g * 2:(g + 1) * 2, :], "ST", waits=w)
            st_tot = st["ST"]
        p.add("sync", lambda s: None, waits=[("ST", st_tot)])

        p.emit_all(block, sems)
    return nc


_PERM_COLS = None


def _perm_cols():
    global _PERM_COLS
    if _PERM_COLS is None:
        idx = np.empty((IC, 128), dtype=np.int64)
        for c in range(IC):
            for pp in range(128):
                idx[c, pp] = 4 * (128 * (c // 4) + pp) + (c % 4)
        _PERM_COLS = idx
    return _PERM_COLS


def _prep_shared(w_qkv, w_out, w_gate, w_up, w_down, decay_param):
    w4 = np.ascontiguousarray(
        w_qkv.reshape(KC, 128, 3, H, 128).transpose(3, 2, 0, 1, 4))
    wo = np.ascontiguousarray(
        w_out.reshape(KC, 128, KC, 128).transpose(2, 0, 1, 3))
    idx = _perm_cols()
    wg = np.ascontiguousarray(
        w_gate[:, idx.reshape(-1)].reshape(KC, 128, IC, 128).transpose(2, 0, 1, 3))
    wu = np.ascontiguousarray(
        w_up[:, idx.reshape(-1)].reshape(KC, 128, IC, 128).transpose(2, 0, 1, 3))
    wd = np.ascontiguousarray(
        w_down[idx.reshape(-1), :].reshape(IC, 128, KC, 128).transpose(2, 0, 1, 3))
    dpbc = np.ascontiguousarray(np.tile(np.asarray(decay_param, np.float32)[None, :],
                                        (128, 1)))
    dp16 = np.ascontiguousarray(np.asarray(decay_param, np.float32).reshape(16, 1))
    iota = np.ascontiguousarray(
        np.tile(np.arange(1, J + 1, dtype=np.float32)[None, :], (128, 1)))
    ehc = np.zeros((H, 128, H), dtype=ml_dtypes.bfloat16)
    for h in range(H):
        ehc[h, :, h] = 1
    ohr = np.zeros((H, H, 128), dtype=ml_dtypes.bfloat16)
    for h in range(H):
        ohr[h, h, :] = 1
    return dict(wqkv=w4, wo=wo, wg=wg, wu=wu, wd=wd, dpbc=dpbc, dp16=dp16,
                iota=iota, ehc=ehc, ohr=ohr)


def kernel(L0_attn_x, L0_attn_w_qkv, L0_attn_w_out, L0_attn_decay_param,
           L0_ffn_w_gate, L0_ffn_w_up, L0_ffn_w_down):
    x = np.asarray(L0_attn_x, np.float32)
    shared = _prep_shared(np.asarray(L0_attn_w_qkv, np.float32),
                          np.asarray(L0_attn_w_out, np.float32),
                          np.asarray(L0_ffn_w_gate, np.float32),
                          np.asarray(L0_ffn_w_up, np.float32),
                          np.asarray(L0_ffn_w_down, np.float32),
                          L0_attn_decay_param)
    in_maps = []
    for c in range(8):
        b, half = c // 2, c % 2
        xs = np.ascontiguousarray(
            x[b, half * J:(half + 1) * J, :].T).reshape(KC, 128, J)
        flag = np.full((128, 1), float(half), np.float32)
        m = dict(shared)
        m["x"] = xs
        m["flag"] = flag
        in_maps.append(m)

    nc = build("full")
    res = run_bass_kernel_spmd(nc, in_maps, core_ids=list(range(8)))
    out = np.empty((4, 2048, 2048), dtype=np.float32)
    for c in range(8):
        b, half = c // 2, c % 2
        r = np.asarray(res.results[c]["out"])
        out[b, half * J:(half + 1) * J, :] = r.reshape(2048, J).T
    return out
